# version 03
# baseline (speedup 1.0000x reference)
import numpy as np
import contextlib
import concourse.bass as bass
import concourse.mybir as mybir
from concourse.bass_utils import run_bass_kernel_spmd

F32 = mybir.dt.float32
I32 = mybir.dt.int32
AF = mybir.ActivationFunctionType
ALU = mybir.AluOpType
AX = mybir.AxisListType.X

NCORE = 8
NSEQ = 4
L = 2048
D = 1024
H = 3072
NT = 16
NG = 4
KC = 8
HC = 24
INV_SQRT_D = 1.0 / 32.0
FFN_BARRIER = False
DOWN_BARRIER = False
EPS = 1e-5

A0 = 0
B0 = 16384
C0 = 32768
P0 = 49152
TB = 51200
IDN = 52224
MSK = 52352
DBF = 52480
BQ0 = 52608
ST0 = 52632
ROWS = 52696
TOT = 52824

FB_WG = B0
FB_WU = B0 + 2048
FB_WD = B0 + 4096
FB_U2 = B0 + 6144
FB_YS = B0 + 10240
FB_GS = B0 + 12288
FB_USQ = B0 + 13312
PB_WP = B0
PB_XIN = B0 + 8192
PB_U = B0 + 10240
PB_USQ = B0 + 11264
PB_BPR = B0 + 12288

S_MNEG = ST0 + 0
S_ACC = ST0 + 1
S_Z = ST0 + 9
S_RZ = ST0 + 10
S_LNG = ST0 + 11
S_AU = ST0 + 12
S_SU = ST0 + 14
S_SQ = ST0 + 15
S_MEAN = ST0 + 16
S_EX2 = ST0 + 17
S_MSQ = ST0 + 18
S_VEPS = ST0 + 19
S_SQR = ST0 + 20
S_RSTD = ST0 + 21

ENGS = ("sync", "gpsimd", "scalar", "vector", "tensor")


class Sem:
    def __init__(self, es, nc, name):
        self.h = es.enter_context(nc.semaphore(name))
        self.n = 0


class P:

    def __init__(self, MM, VE, SC, dma_sems):
        self.q = {e: [] for e in ENGS}
        self._lw = {}
        self._MM, self._VE, self._SC = MM, VE, SC
        self._dma = dma_sems
        self.tp_mm = 0

    def wait(self, e, sem, val=None):
        v = sem.n if val is None else val
        key = (e, id(sem))
        if v <= 0 or self._lw.get(key, -1) >= v:
            return
        self._lw[key] = v
        self.q[e].append(lambda eng, s=sem, vv=v: eng.wait_ge(s.h, vv))

    def emit(self, e, fn, sem=None, n=1, kind=None):
        if kind == "tp":
            for s2 in self._dma + [self._VE, self._SC]:
                self.wait("tensor", s2)
        elif e != "tensor":
            self.wait(e, self._MM, self.tp_mm)
        if sem is None or n is None:
            self.q[e].append(fn)
            return None
        def g(eng, fn=fn, s=sem, nn=n):
            fn(eng).then_inc(s.h, nn)
        self.q[e].append(g)
        sem.n += n
        if kind == "tp":
            self.tp_mm = sem.n
        return sem.n


def _ln_stats_apply(p, sl, u_off, usq_off, u_ve, VE, SC, apply_fn):
    p.wait("scalar", VE, u_ve)
    p.emit("scalar", lambda e, u=u_off, q=usq_off: e.activation(
        sl(q, 1024), sl(u, 1024), AF.Square, accum_out=sl(S_SQ, 1)), SC, 1)
    sq_sc = SC.n
    p.wait("vector", VE, u_ve)
    p.emit("vector", lambda e: e.reduce_sum(sl(S_SU, 1), sl(S_AU, 2), AX), VE, 1)
    p.wait("vector", VE)
    p.emit("vector", lambda e: e.tensor_scalar_mul(sl(S_MEAN, 1), sl(S_SU, 1),
                                                   1.0 / D), VE, 1)
    p.wait("vector", SC, sq_sc)
    p.emit("vector", lambda e: e.tensor_scalar_mul(sl(S_EX2, 1), sl(S_SQ, 1),
                                                   1.0 / D), VE, 1)
    p.wait("vector", VE)
    p.emit("vector", lambda e: e.tensor_tensor(sl(S_MSQ, 1), sl(S_MEAN, 1),
                                               sl(S_MEAN, 1), ALU.mult), VE, 1)
    p.wait("vector", VE)
    p.emit("vector", lambda e: e.scalar_tensor_tensor(
        sl(S_VEPS, 1), sl(S_EX2, 1), EPS, sl(S_MSQ, 1), ALU.add, ALU.subtract), VE, 1)
    p.wait("scalar", VE)
    p.emit("scalar", lambda e: e.activation(sl(S_SQR, 1), sl(S_VEPS, 1), AF.Sqrt),
           SC, 1)
    p.wait("vector", SC)
    p.emit("vector", lambda e: e.reciprocal(sl(S_RSTD, 1), sl(S_SQR, 1)), VE, 1)
    p.wait("vector", VE)
    p.emit("vector", apply_fn, VE, 1)


def build(nseq=NSEQ, debug=False):
    nc = bass.Bass("TRN2", target_bir_lowering=False)
    xin_d = nc.dram_tensor("x_in", [nseq, L, D], F32, kind="ExternalInput").ap()
    wqkv = nc.dram_tensor("wqkv", [D, 3 * D], F32, kind="ExternalInput").ap()
    bq_d = nc.dram_tensor("bq_pp", [128, HC], F32, kind="ExternalInput").ap()
    wproj = nc.dram_tensor("wproj", [D, D], F32, kind="ExternalInput").ap()
    bpr_d = nc.dram_tensor("bproj_row", [1, D], F32, kind="ExternalInput").ap()
    gam_d = nc.dram_tensor("gamma2", [1, 1], F32, kind="ExternalInput").ap()
    wg_d = nc.dram_tensor("wg_i", [H, D], I32, kind="ExternalInput").ap()
    wu_d = nc.dram_tensor("wu_i", [H, D], I32, kind="ExternalInput").ap()
    wd_d = nc.dram_tensor("wd_i", [D, H], I32, kind="ExternalInput").ap()
    id_d = nc.dram_tensor("ident_c", [128, 128], F32, kind="ExternalInput").ap()
    df_d = nc.dram_tensor("diff_c", [128, 128], F32, kind="ExternalInput").ap()
    tr_d = nc.dram_tensor("tri_c", [128, 128], F32, kind="ExternalInput").ap()
    on_d = nc.dram_tensor("ones_c", [1, 128], F32, kind="ExternalInput").ap()
    y_d = nc.dram_tensor("y", [nseq, L, D], F32, kind="ExternalOutput").ap()
    _scr_kind = "ExternalOutput" if debug else "Internal"
    wgT_s = nc.dram_tensor("wgT_s", [HC, 128, D], F32, kind=_scr_kind).ap()
    wuT_s = nc.dram_tensor("wuT_s", [HC, 128, D], F32, kind=_scr_kind).ap()
    wdT_s = nc.dram_tensor("wdT_s", [HC, 128, D], F32, kind=_scr_kind).ap()
    dbg = {}
    if debug:
        for name in ("s1", "s2", "s3", "s4", "s5", "s6", "s7"):
            dbg[name] = nc.dram_tensor("dbg_" + name, [128, TOT], F32,
                                       kind="ExternalOutput").ap()

    with contextlib.ExitStack() as es:
        ar = es.enter_context(nc.sbuf_tensor("arena", [128, TOT], F32))
        ib = es.enter_context(nc.sbuf_tensor("ibuf", [128, 256], I32))
        pS = es.enter_context(nc.psum_tensor("pS", [128, 2048], F32))
        pA = es.enter_context(nc.psum_tensor("pA", [128, 512], F32))
        pB = es.enter_context(nc.psum_tensor("pB", [128, 512], F32))
        pC = es.enter_context(nc.psum_tensor("pC", [128, 512], F32))
        pD = es.enter_context(nc.psum_tensor("pD", [128, 512], F32))
        MM = Sem(es, nc, "MM")
        VE = Sem(es, nc, "VE")
        SC = Sem(es, nc, "SC")
        L0 = Sem(es, nc, "L0")
        DX = [Sem(es, nc, "DX0"), Sem(es, nc, "DX1")]
        DQ = [Sem(es, nc, "DQ0"), Sem(es, nc, "DQ1")]
        DPX = [Sem(es, nc, "DPX0"), Sem(es, nc, "DPX1")]
        DG = [Sem(es, nc, "DG0"), Sem(es, nc, "DG1")]
        DU = [Sem(es, nc, "DU0"), Sem(es, nc, "DU1")]
        DD = [Sem(es, nc, "DD0"), Sem(es, nc, "DD1")]
        GY = [Sem(es, nc, "GY0"), Sem(es, nc, "GY1")]
        GA = [Sem(es, nc, "GA0"), Sem(es, nc, "GA1")]
        GS = [Sem(es, nc, "GS0"), Sem(es, nc, "GS1")]
        SDBG = Sem(es, nc, "SDBG")
        ALL_DMA = [L0] + DX + DQ + DPX + DG + DU + DD + GY + GA + GS + [SDBG]
        blk = es.enter_context(nc.Block())
        p = P(MM, VE, SC, ALL_DMA)

        def sl(off, n):
            return ar[:, off:off + n]

        IDENT = sl(IDN, 128)
        MASK = sl(MSK, 128)
        DIAG = sl(DBF, 128)
        ONES = ar[0:1, ROWS:ROWS + 128]
        GAMR = ar[0:1, DBF + 1:DBF + 2]
        LNGR = ar[0:1, DBF:DBF + 1]
        BPROW = ar[0:1, PB_BPR:PB_BPR + D]

        st = {
            "xev": {0: 0, 1: 0},
            "slab": {0: 0, 1: 0},
            "mmfree": {0: 0, 1: 0},
            "vev": {0: 0, 1: 0},
            "qev": {},
            "ptev": {0: 0, 1: 0},
            "ptr_ve": 0, "pv_mm": 0,
            "opv": {0: 0, 1: 0},
            "ot_mm": 0,
            "otv_v": {0: 0, 1: 0},
            "otv_s": {0: 0, 1: 0},
            "prj": {0: 0, 1: 0},
            "x2t": {0: 0, 1: 0},
            "fslab": {0: 0, 1: 0},
            "dslab": {0: 0, 1: 0},
            "gate": 0, "up": 0,
            "hgs": {0: 0, 1: 0},
            "ffn_ve": 0, "ffn_sc": 0,
            "yst": {0: 0, 1: 0},
            "exp": 0, "t1r": 0,
        }


        def snap(name):
            if not debug:
                return
            for sem in ALL_DMA + [MM, VE, SC]:
                p.wait("gpsimd", sem)
            p.emit("gpsimd", lambda e, nm=name: e.dma_start(
                out=dbg[nm], in_=sl(0, TOT)), SDBG, 16)
            for e2 in ("sync", "scalar", "vector", "tensor"):
                p.wait(e2, SDBG)

        p.emit("sync", lambda e: e.dma_start(out=IDENT, in_=id_d), L0, 16)
        p.emit("sync", lambda e: e.dma_start(out=sl(P0 + 256, 128), in_=df_d), L0, 16)
        p.emit("sync", lambda e: e.dma_start(out=sl(P0 + 384, 128), in_=tr_d), L0, 16)
        p.emit("sync", lambda e: e.dma_start(out=ONES, in_=on_d), L0, 16)
        p.emit("sync", lambda e: e.dma_start(out=GAMR, in_=gam_d), L0, 16)
        p.emit("sync", lambda e: e.dma_start(out=sl(BQ0, HC), in_=bq_d), L0, 16)
        p.wait("scalar", L0)
        p.emit("scalar", lambda e: e.activation(LNGR, GAMR, AF.Ln), SC, 1)
        p.wait("tensor", SC)
        p.emit("tensor", lambda e: e.matmul(pA[:, 0:1], ONES, LNGR, start=True,
                                            stop=True), MM, 1)
        p.wait("vector", MM)
        p.emit("vector", lambda e: e.tensor_copy(sl(S_LNG, 1), pA[:, 0:1]), VE, 1, kind="pr")
        p.wait("scalar", VE)
        p.emit("scalar", lambda e: e.activation(MASK, sl(P0 + 256, 128), AF.Exp,
                                                scale=sl(S_LNG, 1)), SC, 1)
        p.wait("vector", SC)
        p.emit("vector", lambda e: e.tensor_tensor(MASK, MASK, sl(P0 + 384, 128),
                                                   ALU.mult), VE, 1)

        pp = {"cast": {0: 0, 1: 0}, "tp": {0: 0, 1: 0}, "ev": {0: 0, 1: 0},
              "stg": {0: 0, 1: 0}}

        def preproc(w_src, dst, transp_src):
            for c in range(HC):
                stg = B0 + (c % 2) * 1024
                for k in range(KC):
                    buf = (c * KC + k) % 2
                    if transp_src:
                        src = w_src[128 * c:128 * (c + 1), 128 * k:128 * (k + 1)]
                    else:
                        src = w_src[128 * k:128 * (k + 1), 128 * c:128 * (c + 1)]
                    p.wait("gpsimd", VE, pp["cast"][buf])
                    p.emit("gpsimd", lambda e, s2=src, b=buf: e.dma_start(
                        out=ib[:, b * 128:(b + 1) * 128], in_=s2), GA[buf], 16)
                    p.wait("vector", GA[buf])
                    p.wait("vector", MM, pp["tp"][buf])
                    p.wait("vector", GS[c % 2], pp["stg"][c % 2])
                    p.emit("vector", lambda e, b=buf: e.tensor_copy(
                        sl(P0 + b * 128, 128), ib[:, b * 128:(b + 1) * 128]), VE, 1)
                    pp["cast"][buf] = VE.n
                    half = (k // 4) % 2
                    bank = (pC, pD)[half]
                    p.wait("tensor", VE)
                    p.wait("tensor", VE, pp["ev"][half])
                    p.emit("tensor", lambda e, b=buf, bk=bank, k=k: e.transpose(
                        bk[:, (k % 4) * 128:(k % 4 + 1) * 128], sl(P0 + b * 128, 128),
                        IDENT), MM, 1, kind="tp")
                    pp["tp"][buf] = MM.n
                    if k % 4 == 3:
                        p.wait("vector", MM)
                        p.emit("vector", lambda e, h2=half, bk=bank, stg=stg:
                               e.tensor_copy(sl(stg + h2 * 512, 512), bk[:]), VE, 1,
                               kind="pr")
                        pp["ev"][half] = VE.n
                p.wait("gpsimd", VE)
                p.emit("gpsimd", lambda e, c=c, stg=stg, d=dst: e.dma_start(
                    out=d[c], in_=sl(stg, 1024)), GS[c % 2], 16)
                pp["stg"][c % 2] = GS[c % 2].n

        p.wait("vector", SC)
        preproc(wg_d, wgT_s, True)
        preproc(wu_d, wuT_s, True)
        preproc(wd_d, wdT_s, False)
        pre_ve = VE.n

        for s in range(nseq):
            if s > 0:
                for e in ENGS:
                    for sem in ALL_DMA + [MM, VE, SC]:
                        p.wait(e, sem)

            if s == 0:
                p.wait("sync", GS[0])
                p.wait("sync", GS[1])
                p.wait("tensor", VE)
            xg_done = [0] * NG
            for g in range(NG):
                if g > 0:
                    p.wait("sync", MM, xg_done[g - 1])
                for j in range(4):
                    i = g * 4 + j
                    p.emit("sync", lambda e, s=s, i=i, j=j: e.dma_start(
                        out=sl(B0 + j * 1024, 1024),
                        in_=xin_d[s, 128 * i:128 * (i + 1), :]), DX[g % 2], 16)
                p.wait("tensor", DX[g % 2])
                for k in range(KC):
                    bank = (pC, pD)[k % 2]
                    p.wait("tensor", VE, st["xev"][k % 2])
                    for j in range(4):
                        p.emit("tensor", lambda e, bk=bank, j=j, k=k: e.transpose(
                            bk[:, j * 128:(j + 1) * 128],
                            sl(B0 + j * 1024 + k * 128, 128), IDENT),
                            MM, 1 if j == 3 else None, kind="tp")
                    p.wait("vector", MM)
                    p.emit("vector", lambda e, k=k, g=g, bk=bank: e.tensor_copy(
                        sl(A0 + k * 2048 + g * 512, 512), bk[:]), VE, 1, kind="pr")
                    st["xev"][k % 2] = VE.n
                xg_done[g] = MM.n
            xphase_ve = VE.n
            if s == 0:
                snap("s1")

            p.wait("tensor", VE, xphase_ve)
            if s == 0:
                p.wait("sync", VE, pre_ve)
            rot = [0]
            vrot = [0]

            def qkv_slab(c, col_base, sbase):
                wbuf = sbase + (c % 2) * 1024
                p.wait("sync", MM, st["slab"][c % 2])
                for k in range(KC):
                    p.emit("sync", lambda e, k=k, c=c, wb=wbuf, cb=col_base: e.dma_start(
                        out=sl(wb + k * 128, 128),
                        in_=wqkv[128 * k:128 * (k + 1),
                                 cb + 128 * c:cb + 128 * (c + 1)]), DQ[c % 2], 16)
                return wbuf

            def qkv_mms(wbuf, g, bank, bank_ap=None):
                for k in range(KC):
                    ap = bank_ap if bank_ap is not None else (lambda b: b[:])
                    p.emit("tensor", lambda e, bk=bank, wb=wbuf, k=k, g=g, ap=ap:
                           e.matmul(ap(bk), sl(wb + k * 128, 128),
                                    sl(A0 + k * 2048 + g * 512, 512),
                                    start=(k == 0), stop=(k == KC - 1)),
                           MM, 1 if k == KC - 1 else None)

            for part, cb, sbase in (("K", D, C0), ("V", 2 * D, P0)):
                for c in range(KC):
                    wbuf = qkv_slab(c, cb, sbase)
                    p.wait("tensor", DQ[c % 2])
                    for g in range(NG):
                        r = rot[0] % 2
                        bank = (pA, pB)[r]
                        rot[0] += 1
                        p.wait("tensor", VE, st["mmfree"][r])
                        qkv_mms(wbuf, g, bank)
                        p.wait("vector", MM)
                        if part == "K":
                            p.emit("vector", lambda e, bk=bank, c=c, g=g: e.tensor_scalar(
                                sl(B0 + c * 2048 + g * 512, 512), bk[:],
                                sl(BQ0 + KC + c, 1), None, ALU.add), VE, 1, kind="pr")
                            st["mmfree"][r] = VE.n
                        else:
                            p.emit("vector", lambda e, bk=bank, c=c: e.tensor_scalar(
                                sl(TB, 512), bk[:], sl(BQ0 + 2 * KC + c, 1), None,
                                ALU.add), VE, 1, kind="pr")
                            st["mmfree"][r] = VE.n
                            p.wait("tensor", VE)
                            for j in range(4):
                                vb = vrot[0] % 2
                                bank2 = (pC, pD)[vb]
                                p.wait("tensor", SC, st["vev"][vb])
                                p.emit("tensor", lambda e, bk2=bank2, j=j: e.transpose(
                                    bk2[:, 0:128], sl(TB + j * 128, 128), IDENT), MM, 1,
                                    kind="tp")
                                p.wait("scalar", MM)
                                p.emit("scalar", lambda e, bk2=bank2, g=g, j=j, c=c:
                                       e.copy(sl(C0 + (g * 4 + j) * 1024 + c * 128, 128),
                                              bk2[:, 0:128]), SC, 1, kind="pr")
                                st["vev"][vb] = SC.n
                                vrot[0] += 1
                    st["slab"][c % 2] = MM.n
            p.wait("tensor", SC)
            for g in range(NG):
                for c in range(KC):
                    wbuf = qkv_slab(c, 0, P0)
                    p.wait("tensor", DQ[c % 2])
                    if c < 4:
                        bank, bap = pS, (lambda b, c=c: b[:, c * 512:(c + 1) * 512])
                    else:
                        bank, bap = (pA, pB, pC, pD)[c - 4], (lambda b: b[:])
                    p.wait("tensor", VE, st["qev"].get(("v", c), 0))
                    p.wait("tensor", SC, st["qev"].get(("s", c), 0))
                    qkv_mms(wbuf, g, bank, bap)
                    st["slab"][c % 2] = MM.n
                qmm = MM.n
                p.wait("vector", MM, qmm)
                p.wait("scalar", MM, qmm)
                for c in range(KC):
                    if c < 4:
                        src = pS[:, c * 512:(c + 1) * 512]
                    else:
                        src = (pA, pB, pC, pD)[c - 4][:]
                    if c % 2 == 0:
                        p.emit("vector", lambda e, src=src, c=c, g=g: e.tensor_scalar(
                            sl(A0 + c * 2048 + g * 512, 512), src,
                            sl(BQ0 + c, 1), None, ALU.add), VE, 1, kind="pr")
                        st["qev"][("v", c)] = VE.n
                    else:
                        p.emit("scalar", lambda e, src=src, c=c, g=g: e.activation(
                            sl(A0 + c * 2048 + g * 512, 512), src, AF.Identity,
                            bias=sl(BQ0 + c, 1), scale=1.0), SC, 1, kind="pr")
                        st["qev"][("s", c)] = SC.n
            qkv_ve = VE.n
            qkv_sc = SC.n
            if s == 0:
                snap("s2")

            p.wait("tensor", VE, qkv_ve)
            p.wait("tensor", SC, qkv_sc)
            for i in range(NT):
                kb = i // 4
                lo = (i % 4) * 128
                p.wait("tensor", SC, st["exp"])
                p.wait("tensor", VE, st["t1r"])
                for kg in range(4):
                    for k in range(KC):
                        p.emit("tensor", lambda e, kg=kg, k=k, i=i: e.matmul(
                            pS[:, kg * 512:(kg + 1) * 512],
                            sl(A0 + k * 2048 + i * 128, 128),
                            sl(B0 + k * 2048 + kg * 512, 512),
                            start=(k == 0), stop=(k == KC - 1)),
                            MM, 1 if (k == KC - 1 and kg == 3) else None)
                s_done = MM.n
                p.wait("vector", MM, s_done)
                p.wait("vector", SC, st["exp"])
                p.emit("vector", lambda e: e.reduce_max(sl(S_MNEG, 1), pS[:], AX,
                                                        negate=True), VE, 1, kind="pr")
                p.wait("vector", VE)
                p.emit("vector", lambda e: e.tensor_scalar(
                    sl(S_MNEG, 1), sl(S_MNEG, 1), INV_SQRT_D, 0.0, ALU.mult, ALU.min),
                    VE, 1)
                p.emit("vector", lambda e, i=i: e.scalar_tensor_tensor(
                    DIAG, pS[:, i * 128:(i + 1) * 128], INV_SQRT_D, MASK,
                    ALU.mult, ALU.mult), VE, 1, kind="pr")
                p.emit("vector", lambda e: e.memset(sl(S_ACC, 8), 0.0), VE, 1)
                stats_ve = VE.n
                st["t1r"] = stats_ve
                p.wait("scalar", VE, stats_ve)
                p.wait("scalar", VE, st["ptr_ve"])
                p.wait("scalar", MM, st["pv_mm"])
                na = 0
                for b in range(4):
                    if b != kb:
                        p.emit("scalar", lambda e, b=b, na=na: e.activation(
                            sl(P0 + b * 512, 512), pS[:, b * 512:(b + 1) * 512], AF.Exp,
                            bias=sl(S_MNEG, 1), scale=INV_SQRT_D,
                            accum_out=sl(S_ACC + na, 1)), SC, 1, kind="pr")
                        na += 1
                if lo > 0:
                    p.emit("scalar", lambda e, kb=kb, lo=lo, na=na: e.activation(
                        sl(P0 + kb * 512, lo), pS[:, kb * 512:kb * 512 + lo], AF.Exp,
                        bias=sl(S_MNEG, 1), scale=INV_SQRT_D,
                        accum_out=sl(S_ACC + na, 1)), SC, 1, kind="pr")
                    na += 1
                if lo < 384:
                    p.emit("scalar", lambda e, kb=kb, lo=lo, na=na: e.activation(
                        sl(P0 + kb * 512 + lo + 128, 384 - lo),
                        pS[:, kb * 512 + lo + 128:(kb + 1) * 512], AF.Exp,
                        bias=sl(S_MNEG, 1), scale=INV_SQRT_D,
                        accum_out=sl(S_ACC + na, 1)), SC, 1, kind="pr")
                    na += 1
                p.emit("scalar", lambda e, i=i, na=na: e.activation(
                    sl(P0 + i * 128, 128), DIAG, AF.Exp, bias=sl(S_MNEG, 1), scale=1.0,
                    accum_out=sl(S_ACC + na, 1)), SC, 1)
                st["exp"] = SC.n
                p.wait("vector", SC, st["exp"])
                p.emit("vector", lambda e: e.reduce_sum(sl(S_Z, 1), sl(S_ACC, 8), AX),
                       VE, 1)
                p.wait("vector", VE)
                p.emit("vector", lambda e: e.reciprocal(sl(S_RZ, 1), sl(S_Z, 1)), VE, 1)
                rz_ve = VE.n
                if s == 0 and i == 0:
                    snap("s3")
                p.wait("tensor", SC, st["exp"])
                for k in range(NT):
                    half = (k // 4) % 2
                    bank = (pC, pD)[half]
                    if k % 4 == 0:
                        p.wait("tensor", VE, st["ptev"][half])
                    p.emit("tensor", lambda e, bk=bank, k=k: e.transpose(
                        bk[:, (k % 4) * 128:(k % 4 + 1) * 128], sl(P0 + k * 128, 128),
                        IDENT), MM, 1 if k % 4 == 3 else None, kind="tp")
                    if k % 4 == 3:
                        p.wait("vector", MM)
                        p.emit("vector", lambda e, bk=bank, k=k: e.tensor_copy(
                            sl(P0 + (k - 3) * 128, 512), bk[:]), VE, 1, kind="pr")
                        st["ptev"][half] = VE.n
                st["ptr_ve"] = VE.n
                ptev_ve = VE.n
                p.wait("tensor", VE, ptev_ve)
                for dg in range(2):
                    bank = (pA, pB)[dg]
                    p.wait("tensor", VE, st["opv"][dg])
                    for k in range(NT):
                        p.emit("tensor", lambda e, bk=bank, k=k, dg=dg: e.matmul(
                            bk[:], sl(P0 + k * 128, 128),
                            sl(C0 + k * 1024 + dg * 512, 512),
                            start=(k == 0), stop=(k == NT - 1)),
                            MM, 1 if k == NT - 1 else None)
                st["pv_mm"] = MM.n
                p.wait("vector", MM, st["pv_mm"])
                p.wait("vector", VE, rz_ve)
                p.wait("vector", MM, st["ot_mm"])
                for dg in range(2):
                    bank = (pA, pB)[dg]
                    p.emit("vector", lambda e, bk=bank, dg=dg: e.tensor_scalar(
                        sl(TB + dg * 512, 512), bk[:], sl(S_RZ, 1), None, ALU.mult),
                        VE, 1, kind="pr")
                    st["opv"][dg] = VE.n
                oev_ve = VE.n
                if s == 0 and i == 0:
                    snap("s4")
                p.wait("tensor", VE, oev_ve)
                for k in range(KC):
                    bank = (pC, pD)[k % 2]
                    p.wait("tensor", VE, st["otv_v"][k % 2])
                    p.wait("tensor", SC, st["otv_s"][k % 2])
                    p.wait("tensor", VE, st["ptev"][k % 2])
                    p.emit("tensor", lambda e, bk=bank, k=k: e.transpose(
                        bk[:, 0:128], sl(TB + k * 128, 128), IDENT), MM, 1, kind="tp")
                    if k % 2 == 0:
                        p.wait("vector", MM)
                        p.emit("vector", lambda e, bk=bank, k=k, i=i: e.tensor_copy(
                            sl(A0 + k * 2048 + i * 128, 128), bk[:, 0:128]), VE, 1,
                            kind="pr")
                        st["otv_v"][k % 2] = VE.n
                    else:
                        p.wait("scalar", MM)
                        p.emit("scalar", lambda e, bk=bank, k=k, i=i: e.copy(
                            sl(A0 + k * 2048 + i * 128, 128), bk[:, 0:128]), SC, 1,
                            kind="pr")
                        st["otv_s"][k % 2] = SC.n
                st["ot_mm"] = MM.n
            att_mm = MM.n
            att_ve = VE.n
            att_sc = SC.n

            p.wait("sync", MM, att_mm)
            for k in range(KC):
                p.emit("sync", lambda e, k=k: e.dma_start(
                    out=sl(PB_WP + k * 1024, 1024),
                    in_=wproj[128 * k:128 * (k + 1), :]), L0, 16)
            p.emit("sync", lambda e: e.dma_start(out=BPROW, in_=bpr_d), L0, 16)
            wproj_ld = L0.n
            p.wait("tensor", SC, att_sc)
            p.wait("tensor", VE, att_ve)
            xin_free = {0: 0, 1: 0}
            for i in range(NT):
                xb = i % 2
                p.wait("sync", VE, xin_free[xb])
                p.emit("sync", lambda e, s=s, i=i, xb=xb: e.dma_start(
                    out=sl(PB_XIN + xb * 1024, 1024),
                    in_=xin_d[s, 128 * i:128 * (i + 1), :]), DPX[xb], 16)
                xld = DPX[xb].n
                p.wait("tensor", L0, wproj_ld)
                for og in range(2):
                    bank = (pA, pB)[og]
                    p.wait("tensor", VE, st["prj"][og])
                    for k in range(KC):
                        p.emit("tensor", lambda e, bk=bank, k=k, i=i, og=og: e.matmul(
                            bk[:], sl(A0 + k * 2048 + i * 128, 128),
                            sl(PB_WP + k * 1024 + og * 512, 512),
                            start=(k == 0), stop=False))
                    p.emit("tensor", lambda e, bk=bank, og=og: e.matmul(
                        bk[:], ONES, BPROW[0:1, og * 512:(og + 1) * 512],
                        start=False, stop=True), MM, 1)
                r_mm = MM.n
                p.wait("vector", MM, r_mm)
                p.wait("vector", DPX[xb], xld)
                p.wait("vector", VE, st.get("lnap", 0))
                for og in range(2):
                    bank = (pA, pB)[og]
                    p.emit("vector", lambda e, bk=bank, og=og, xb=xb:
                           e.scalar_tensor_tensor(
                               sl(PB_U + og * 512, 512), bk[:], 1.0,
                               sl(PB_XIN + xb * 1024 + og * 512, 512),
                               ALU.mult, ALU.add, accum_out=sl(S_AU + og, 1)), VE, 1,
                           kind="pr")
                    st["prj"][og] = VE.n
                xin_free[xb] = VE.n
                u_ve = VE.n
                _ln_stats_apply(p, sl, PB_U, PB_USQ, u_ve, VE, SC,
                                lambda e, i=i: e.tensor_scalar(
                                    sl(C0 + i * 1024, 1024), sl(PB_U, 1024),
                                    sl(S_MEAN, 1), sl(S_RSTD, 1),
                                    ALU.subtract, ALU.mult))
                st["lnap"] = VE.n
            proj_ve = VE.n
            proj_mm = MM.n
            proj_sc = SC.n
            if s == 0:
                snap("s5")

            p.wait("sync", MM, proj_mm)
            for tg in range(NG):
                p.wait("tensor", VE, proj_ve if tg == 0 else st["ffn_ve"])
                p.wait("tensor", SC, proj_sc if tg == 0 else st["ffn_sc"])
                for k in range(KC):
                    bank = (pC, pD)[k % 2]
                    p.wait("tensor", VE, st["x2t"][k % 2])
                    for j in range(4):
                        p.emit("tensor", lambda e, bk=bank, tg=tg, j=j, k=k: e.transpose(
                            bk[:, j * 128:(j + 1) * 128],
                            sl(C0 + (tg * 4 + j) * 1024 + k * 128, 128), IDENT),
                            MM, 1 if j == 3 else None, kind="tp")
                    p.wait("vector", MM)
                    p.emit("vector", lambda e, bk=bank, k=k: e.tensor_copy(
                        sl(A0 + 12288 + k * 512, 512), bk[:]), VE, 1, kind="pr")
                    st["x2t"][k % 2] = VE.n
                x2t_ve = VE.n
                p.wait("tensor", VE, x2t_ve)
                for c in range(HC):
                    wb = c % 2
                    p.wait("sync", MM, st["fslab"][wb])
                    p.emit("sync", lambda e, c=c, wb=wb: e.dma_start(
                        out=sl(FB_WG + wb * 1024, 1024), in_=wgT_s[c]), DG[wb], 16)
                    p.emit("sync", lambda e, c=c, wb=wb: e.dma_start(
                        out=sl(FB_WU + wb * 1024, 1024), in_=wuT_s[c]), DU[wb], 16)
                    p.wait("tensor", DG[wb])
                    p.wait("tensor", DU[wb])
                    p.wait("tensor", SC, st["gate"])
                    for k in range(KC):
                        p.emit("tensor", lambda e, wb=wb, k=k: e.matmul(
                            pA[:], sl(FB_WG + wb * 1024 + k * 128, 128),
                            sl(A0 + 12288 + k * 512, 512),
                            start=(k == 0), stop=(k == KC - 1)),
                            MM, 1 if k == KC - 1 else None)
                    gmm = MM.n
                    p.wait("tensor", VE, st["up"])
                    for k in range(KC):
                        p.emit("tensor", lambda e, wb=wb, k=k: e.matmul(
                            pB[:], sl(FB_WU + wb * 1024 + k * 128, 128),
                            sl(A0 + 12288 + k * 512, 512),
                            start=(k == 0), stop=(k == KC - 1)),
                            MM, 1 if k == KC - 1 else None)
                    st["fslab"][wb] = MM.n
                    umm = MM.n
                    if debug and s == 0 and tg == 0 and c == 7:
                        p.wait("vector", MM, umm)
                        p.emit("vector", lambda e: e.tensor_copy(
                            sl(FB_USQ, 512), pA[:]), VE, 1, kind="pr")
                        p.emit("vector", lambda e: e.tensor_copy(
                            sl(FB_USQ + 512, 512), pB[:]), VE, 1, kind="pr")
                        p.emit("vector", lambda e, wb=wb: e.tensor_copy(
                            sl(P0, 1024), sl(FB_WG + wb * 1024, 1024)), VE, 1)
                        p.emit("vector", lambda e, wb=wb: e.tensor_copy(
                            sl(P0 + 1024, 1024), sl(FB_WU + wb * 1024, 1024)), VE, 1)
                        p.wait("tensor", VE)
                    gsb = c % 2
                    p.wait("scalar", MM, gmm)
                    p.wait("scalar", VE, st["hgs"][gsb])
                    p.emit("scalar", lambda e, gsb=gsb: e.activation(
                        sl(FB_GS + gsb * 512, 512), pA[:], AF.Silu,
                        scale=1.0 / 127.0), SC, 1, kind="pr")
                    st["gate"] = SC.n
                    p.wait("vector", MM, umm)
                    p.wait("vector", SC, st["gate"])
                    p.emit("vector", lambda e, c=c, gsb=gsb: e.tensor_tensor(
                        sl(A0 + c * 512, 512), sl(FB_GS + gsb * 512, 512), pB[:],
                        ALU.mult), VE, 1, kind="pr")
                    st["up"] = VE.n
                    st["hgs"][gsb] = VE.n
                    if FFN_BARRIER:
                        for e2 in ENGS:
                            for sem2 in ALL_DMA + [MM, VE, SC]:
                                p.wait(e2, sem2)
                h_ve = VE.n
                if s == 0 and tg == 0:
                    snap("s6")
                p.wait("tensor", VE, h_ve)
                for c in range(HC):
                    wb = c % 2
                    p.wait("sync", MM, st["dslab"][wb])
                    p.emit("sync", lambda e, c=c, wb=wb: e.dma_start(
                        out=sl(FB_WD + wb * 1024, 1024), in_=wdT_s[c]), DD[wb], 16)
                    p.wait("tensor", DD[wb])
                    for qt in range(4):
                        for dg in range(2):
                            p.emit("tensor", lambda e, c=c, wb=wb, qt=qt, dg=dg: e.matmul(
                                (pS[:, qt * 512:(qt + 1) * 512] if dg == 0 else
                                 (pA, pB, pC, pD)[qt][:]),
                                sl(A0 + c * 512 + qt * 128, 128),
                                sl(FB_WD + wb * 1024 + dg * 512, 512),
                                start=(c == 0), stop=(c == HC - 1),
                                skip_group_check=True),
                                MM, 1 if (qt == 3 and dg == 1) else None)
                    if DOWN_BARRIER:
                        for e2 in ENGS:
                            for sem2 in ALL_DMA + [MM, VE, SC]:
                                p.wait(e2, sem2)
                    st["dslab"][wb] = MM.n
                    if debug and s == 0 and tg == 0 and c == 0:
                        p.wait("vector", MM)
                        p.emit("vector", lambda e: e.tensor_copy(
                            sl(P0, 1024), sl(FB_WD, 1024)), VE, 1)
                        p.wait("tensor", VE)
                down_mm = MM.n
                if debug and s == 0 and tg == 0:
                    p.wait("vector", MM, down_mm)
                    p.emit("vector", lambda e: e.tensor_copy(
                        sl(P0 + 1024, 512), pS[:, 0:512]), VE, 1)
                    p.emit("vector", lambda e: e.tensor_copy(
                        sl(P0 + 1536, 512), pA[:]), VE, 1)
                    p.wait("tensor", VE)
                p.wait("vector", MM, down_mm)
                for qt in range(4):
                    ti = tg * 4 + qt
                    p.wait("vector", VE, st.get("lnap2", 0))
                    for dg in range(2):
                        src = (pS[:, qt * 512:(qt + 1) * 512] if dg == 0 else
                               (pA, pB, pC, pD)[qt][:])
                        p.emit("vector", lambda e, src=src, qt=qt, dg=dg, ti=ti:
                               e.scalar_tensor_tensor(
                                   sl(FB_U2 + qt * 1024 + dg * 512, 512), src,
                                   1.0 / (127.0 * 127.0),
                                   sl(C0 + ti * 1024 + dg * 512, 512),
                                   ALU.mult, ALU.add, accum_out=sl(S_AU + dg, 1)),
                               VE, 1, kind="pr")
                    u2_ve = VE.n
                    yb = qt % 2
                    p.wait("vector", GY[yb], st["yst"][yb])

                    def apply_y(e, qt=qt, yb=yb):
                        return e.tensor_scalar(
                            sl(FB_YS + yb * 1024, 1024), sl(FB_U2 + qt * 1024, 1024),
                            sl(S_MEAN, 1), sl(S_RSTD, 1), ALU.subtract, ALU.mult)

                    _ln_stats_apply(p, sl, FB_U2 + qt * 1024, FB_USQ, u2_ve, VE, SC,
                                    apply_y)
                    st["lnap2"] = VE.n
                    p.wait("gpsimd", VE)
                    p.emit("gpsimd", lambda e, s=s, ti=ti, yb=yb: e.dma_start(
                        out=y_d[s, 128 * ti:128 * (ti + 1), :],
                        in_=sl(FB_YS + yb * 1024, 1024)), GY[yb], 16)
                    st["yst"][yb] = GY[yb].n
                st["ffn_ve"] = VE.n
                st["ffn_sc"] = SC.n
                if s == 0 and tg == 0:
                    snap("s7")

        for sem in (GY[0], GY[1], GS[0], GS[1], SDBG):
            p.wait("gpsimd", sem)

        @blk.sync
        def _(eng):
            for fn in p.q["sync"]:
                fn(eng)

        @blk.gpsimd
        def _(eng):
            for fn in p.q["gpsimd"]:
                fn(eng)

        @blk.scalar
        def _(eng):
            for fn in p.q["scalar"]:
                fn(eng)

        @blk.vector
        def _(eng):
            for fn in p.q["vector"]:
                fn(eng)

        @blk.tensor
        def _(eng):
            for fn in p.q["tensor"]:
                fn(eng)

    return nc


def kernel(L0_attn_x, L0_attn_Wqkv, L0_attn_bqkv, L0_attn_Wproj, L0_attn_bproj,
           L0_attn_gamma, L0_ffn_w_gate, L0_ffn_w_up, L0_ffn_w_down):
    x = np.ascontiguousarray(np.asarray(L0_attn_x, dtype=np.float32))
    wqkv = np.ascontiguousarray(np.asarray(L0_attn_Wqkv, dtype=np.float32))
    bq = np.asarray(L0_attn_bqkv, dtype=np.float32)
    wp = np.ascontiguousarray(np.asarray(L0_attn_Wproj, dtype=np.float32))
    bp = np.asarray(L0_attn_bproj, dtype=np.float32)
    gam = np.asarray(L0_attn_gamma, dtype=np.float32).reshape(1, 1)
    wg = np.ascontiguousarray(np.asarray(L0_ffn_w_gate, dtype=np.int32))
    wu = np.ascontiguousarray(np.asarray(L0_ffn_w_up, dtype=np.int32))
    wd = np.ascontiguousarray(np.asarray(L0_ffn_w_down, dtype=np.int32))

    bq_pp = np.ascontiguousarray(bq.reshape(HC, 128).T)
    bpr = np.ascontiguousarray(bp.reshape(1, D))
    ident = np.eye(128, dtype=np.float32)
    ii = np.arange(128, dtype=np.float32)
    diff = (ii[:, None] - ii[None, :]).astype(np.float32)
    tri = (diff >= 0).astype(np.float32)
    ones_r = np.ones((1, 128), dtype=np.float32)

    nc = build(NSEQ)
    per = x.shape[0] // NCORE
    in_maps = []
    for c in range(NCORE):
        in_maps.append({
            "x_in": x[c * per:(c + 1) * per],
            "wqkv": wqkv, "bq_pp": bq_pp, "wproj": wp, "bproj_row": bpr,
            "gamma2": gam, "wg_i": wg, "wu_i": wu, "wd_i": wd,
            "ident_c": ident, "diff_c": diff, "tri_c": tri, "ones_c": ones_r,
        })
    res = run_bass_kernel_spmd(nc, in_maps, core_ids=list(range(NCORE)))
    out = np.concatenate([r["y"] for r in res.results], axis=0)
    return np.ascontiguousarray(out.astype(np.float32))


# version 05
# speedup vs baseline: 1.3631x; 1.3631x over previous
import numpy as np
import contextlib
import concourse.bass as bass
import concourse.mybir as mybir
from concourse.bass_utils import run_bass_kernel_spmd

F32 = mybir.dt.float32
FP16 = mybir.dt.float16
I32 = mybir.dt.int32
AF = mybir.ActivationFunctionType
ALU = mybir.AluOpType
AX = mybir.AxisListType.X

NCORE = 8
NSEQ = 4
L = 2048
D = 1024
H = 3072
NT = 16
NG = 4
KC = 8
HC = 24
INV_SQRT_D = 1.0 / 32.0
FFN_BARRIER = False
TPX_EXCL = False
DOWN_BARRIER = False
EPS = 1e-5

A0 = 0
B0 = 16384
C0 = 32768
P0 = 49152
TB = 51200
IDN = 52224
MSK = 52352
DBF = 52480
BQ0 = 52608
ST0 = 52632
ROWS = 52696
TOT = 52824

FH_WG = 0
FH_WU = 2048
FH_WD = 4096
FH_SH = 6144
FH_SL = 7168
FB_U2 = B0 + 4096
FB_YS = B0 + 8192
FB_GS = B0 + 10240
FB_USQ = B0 + 11264
FB_H32 = B0 + 12288
AH_HHI = 0
AH_HLO = 12288
AH_XTH = 24576
AH_XTL = 28672
PB_WP = B0
PB_XIN = B0 + 8192
PB_U = B0 + 10240
PB_USQ = B0 + 11264
PB_BPR = B0 + 12288

S_MNEG = ST0 + 0
S_ACC = ST0 + 1
S_Z = ST0 + 9
S_RZ = ST0 + 10
S_LNG = ST0 + 11
S_AU = ST0 + 12
S_SU = ST0 + 14
S_SQ = ST0 + 15
S_MEAN = ST0 + 16
S_EX2 = ST0 + 17
S_MSQ = ST0 + 18
S_VEPS = ST0 + 19
S_SQR = ST0 + 20
S_RSTD = ST0 + 21

ENGS = ("sync", "gpsimd", "scalar", "vector", "tensor")


class Sem:
    def __init__(self, es, nc, name):
        self.h = es.enter_context(nc.semaphore(name))
        self.n = 0


class P:

    def __init__(self, MM, VE, SC, dma_sems):
        self.q = {e: [] for e in ENGS}
        self._lw = {}
        self._MM, self._VE, self._SC = MM, VE, SC
        self._dma = dma_sems
        self.tp_mm = 0

    def wait(self, e, sem, val=None):
        v = sem.n if val is None else val
        key = (e, id(sem))
        if v <= 0 or self._lw.get(key, -1) >= v:
            return
        self._lw[key] = v
        self.q[e].append(lambda eng, s=sem, vv=v: eng.wait_ge(s.h, vv))

    def emit(self, e, fn, sem=None, n=1, kind=None):
        if kind == "tp":
            for s2 in self._dma + [self._VE, self._SC]:
                self.wait("tensor", s2)
        elif e != "tensor":
            self.wait(e, self._MM, self.tp_mm)
        if sem is None or n is None:
            self.q[e].append(fn)
            return None
        def g(eng, fn=fn, s=sem, nn=n):
            fn(eng).then_inc(s.h, nn)
        self.q[e].append(g)
        sem.n += n
        if kind == "tp":
            self.tp_mm = sem.n
        return sem.n


def _ln_stats_apply(p, sl, u_off, usq_off, u_ve, VE, SC, apply_fn):
    p.wait("scalar", VE, u_ve)
    p.emit("scalar", lambda e, u=u_off, q=usq_off: e.activation(
        sl(q, 1024), sl(u, 1024), AF.Square, accum_out=sl(S_SQ, 1)), SC, 1)
    sq_sc = SC.n
    p.wait("vector", VE, u_ve)
    p.emit("vector", lambda e: e.reduce_sum(sl(S_SU, 1), sl(S_AU, 2), AX), VE, 1)
    p.wait("vector", VE)
    p.emit("vector", lambda e: e.tensor_scalar_mul(sl(S_MEAN, 1), sl(S_SU, 1),
                                                   1.0 / D), VE, 1)
    p.wait("vector", SC, sq_sc)
    p.emit("vector", lambda e: e.tensor_scalar_mul(sl(S_EX2, 1), sl(S_SQ, 1),
                                                   1.0 / D), VE, 1)
    p.wait("vector", VE)
    p.emit("vector", lambda e: e.tensor_tensor(sl(S_MSQ, 1), sl(S_MEAN, 1),
                                               sl(S_MEAN, 1), ALU.mult), VE, 1)
    p.wait("vector", VE)
    p.emit("vector", lambda e: e.scalar_tensor_tensor(
        sl(S_VEPS, 1), sl(S_EX2, 1), EPS, sl(S_MSQ, 1), ALU.add, ALU.subtract), VE, 1)
    p.wait("scalar", VE)
    p.emit("scalar", lambda e: e.activation(sl(S_SQR, 1), sl(S_VEPS, 1), AF.Sqrt),
           SC, 1)
    p.wait("vector", SC)
    p.emit("vector", lambda e: e.reciprocal(sl(S_RSTD, 1), sl(S_SQR, 1)), VE, 1)
    p.wait("vector", VE)
    p.emit("vector", apply_fn, VE, 1)


def build(nseq=NSEQ, debug=False, scrdbg=False):
    nc = bass.Bass("TRN2", target_bir_lowering=False)
    xin_d = nc.dram_tensor("x_in", [nseq, L, D], F32, kind="ExternalInput").ap()
    wqkv = nc.dram_tensor("wqkv", [D, 3 * D], F32, kind="ExternalInput").ap()
    bq_d = nc.dram_tensor("bq_pp", [128, HC], F32, kind="ExternalInput").ap()
    wproj = nc.dram_tensor("wproj", [D, D], F32, kind="ExternalInput").ap()
    bpr_d = nc.dram_tensor("bproj_row", [1, D], F32, kind="ExternalInput").ap()
    gam_d = nc.dram_tensor("gamma2", [1, 1], F32, kind="ExternalInput").ap()
    wg_d = nc.dram_tensor("wg_i", [H, D], I32, kind="ExternalInput").ap()
    wu_d = nc.dram_tensor("wu_i", [H, D], I32, kind="ExternalInput").ap()
    wd_d = nc.dram_tensor("wd_i", [D, H], I32, kind="ExternalInput").ap()
    id_d = nc.dram_tensor("ident_c", [128, 128], F32, kind="ExternalInput").ap()
    df_d = nc.dram_tensor("diff_c", [128, 128], F32, kind="ExternalInput").ap()
    tr_d = nc.dram_tensor("tri_c", [128, 128], F32, kind="ExternalInput").ap()
    on_d = nc.dram_tensor("ones_c", [1, 128], F32, kind="ExternalInput").ap()
    y_d = nc.dram_tensor("y", [nseq, L, D], F32, kind="ExternalOutput").ap()
    _scr_kind = "ExternalOutput" if (debug or scrdbg) else "Internal"
    wgT_s = nc.dram_tensor("wgT_s", [HC, 128, D], FP16, kind=_scr_kind).ap()
    wuT_s = nc.dram_tensor("wuT_s", [HC, 128, D], FP16, kind=_scr_kind).ap()
    wdT_s = nc.dram_tensor("wdT_s", [HC, 128, D], FP16, kind=_scr_kind).ap()
    dbg = {}
    if debug:
        for name in ("s1", "s2", "s3", "s4", "s5", "s6", "s7", "s8"):
            dbg[name] = nc.dram_tensor("dbg_" + name, [128, TOT], F32,
                                       kind="ExternalOutput").ap()

    with contextlib.ExitStack() as es:
        ar = es.enter_context(nc.sbuf_tensor("arena", [128, TOT], F32))
        _base = nc.lookup_mloc(ar).addr
        arA = nc.alloc_sbuf_tensor_at("arA16", [128, 32768], FP16,
                                      offset=_base + A0 * 4).ap()
        arB = nc.alloc_sbuf_tensor_at("arB16", [128, 32768], FP16,
                                      offset=_base + B0 * 4).ap()
        pS = es.enter_context(nc.psum_tensor("pS", [128, 2048], F32))
        pA = es.enter_context(nc.psum_tensor("pA", [128, 512], F32))
        pB = es.enter_context(nc.psum_tensor("pB", [128, 512], F32))
        pC = es.enter_context(nc.psum_tensor("pC", [128, 512], F32))
        pD = es.enter_context(nc.psum_tensor("pD", [128, 512], F32))
        MM = Sem(es, nc, "MM")
        VE = Sem(es, nc, "VE")
        SC = Sem(es, nc, "SC")
        L0 = Sem(es, nc, "L0")
        DX = [Sem(es, nc, "DX0"), Sem(es, nc, "DX1")]
        DQ = [Sem(es, nc, "DQ0"), Sem(es, nc, "DQ1")]
        DPX = [Sem(es, nc, "DPX0"), Sem(es, nc, "DPX1")]
        DG = [Sem(es, nc, "DG0"), Sem(es, nc, "DG1")]
        DU = [Sem(es, nc, "DU0"), Sem(es, nc, "DU1")]
        DD = [Sem(es, nc, "DD0"), Sem(es, nc, "DD1")]
        GY = [Sem(es, nc, "GY0"), Sem(es, nc, "GY1")]
        GA = [Sem(es, nc, "GA0"), Sem(es, nc, "GA1")]
        GS = [Sem(es, nc, "GS0"), Sem(es, nc, "GS1")]
        TP = [Sem(es, nc, "TP0"), Sem(es, nc, "TP1")]
        TQ = [Sem(es, nc, "TQ0"), Sem(es, nc, "TQ1")]
        SDBG = Sem(es, nc, "SDBG")
        ALL_DMA = ([L0] + DX + DQ + DPX + DG + DU + DD + GY + GA + GS + TP + TQ
                   + [SDBG])
        blk = es.enter_context(nc.Block())
        p = P(MM, VE, SC, ALL_DMA)

        def sl(off, n):
            return ar[:, off:off + n]

        def sa(off, n):
            return arA[:, off:off + n]

        def sb(off, n):
            return arB[:, off:off + n]

        IDENT = sl(IDN, 128)
        MASK = sl(MSK, 128)
        DIAG = sl(DBF, 128)
        ONES = ar[0:1, ROWS:ROWS + 128]
        GAMR = ar[0:1, DBF + 1:DBF + 2]
        LNGR = ar[0:1, DBF:DBF + 1]
        BPROW = ar[0:1, PB_BPR:PB_BPR + D]

        st = {
            "xev": {0: 0, 1: 0},
            "slab": {0: 0, 1: 0},
            "mmfree": {0: 0, 1: 0},
            "vev": {0: 0, 1: 0},
            "qev": {},
            "ptev": {0: 0, 1: 0},
            "ptr_ve": 0, "pv_mm": 0,
            "opv": {0: 0, 1: 0},
            "ot_mm": 0,
            "otv_v": {0: 0, 1: 0},
            "otv_s": {0: 0, 1: 0},
            "prj": {0: 0, 1: 0},
            "x2t": {0: 0, 1: 0},
            "fslab": {0: 0, 1: 0},
            "dslab": {0: 0, 1: 0},
            "gate": 0, "up": 0, "gu_mm": 0, "x2s": (0, 0),
            "h32r": {0: 0, 1: 0},
            "hgs": {0: 0, 1: 0},
            "ffn_ve": 0, "ffn_sc": 0,
            "yst": {0: 0, 1: 0},
            "exp": 0, "t1r": 0,
        }


        def snap(name):
            if not debug:
                return
            for sem in ALL_DMA + [MM, VE, SC]:
                p.wait("gpsimd", sem)
            p.emit("gpsimd", lambda e, nm=name: e.dma_start(
                out=dbg[nm], in_=sl(0, TOT)), SDBG, 16)
            for e2 in ("sync", "scalar", "vector", "tensor"):
                p.wait(e2, SDBG)

        p.emit("sync", lambda e: e.dma_start(out=IDENT, in_=id_d), L0, 16)
        p.emit("sync", lambda e: e.dma_start(out=sl(P0 + 256, 128), in_=df_d), L0, 16)
        p.emit("sync", lambda e: e.dma_start(out=sl(P0 + 384, 128), in_=tr_d), L0, 16)
        p.emit("sync", lambda e: e.dma_start(out=ONES, in_=on_d), L0, 16)
        p.emit("sync", lambda e: e.dma_start(out=GAMR, in_=gam_d), L0, 16)
        p.emit("sync", lambda e: e.dma_start(out=sl(BQ0, HC), in_=bq_d), L0, 16)
        p.wait("scalar", L0)
        p.emit("scalar", lambda e: e.activation(LNGR, GAMR, AF.Ln), SC, 1)
        p.wait("tensor", SC)
        p.emit("tensor", lambda e: e.matmul(pA[:, 0:1], ONES, LNGR, start=True,
                                            stop=True), MM, 1)
        p.wait("vector", MM)
        p.emit("vector", lambda e: e.tensor_copy(sl(S_LNG, 1), pA[:, 0:1]), VE, 1, kind="pr")
        p.wait("scalar", VE)
        p.emit("scalar", lambda e: e.activation(MASK, sl(P0 + 256, 128), AF.Exp,
                                                scale=sl(S_LNG, 1)), SC, 1)
        p.wait("vector", SC)
        p.emit("vector", lambda e: e.tensor_tensor(MASK, MASK, sl(P0 + 384, 128),
                                                   ALU.mult), VE, 1)

        pp = {"cast": {0: 0, 1: 0}, "tp": {0: 0, 1: 0}, "stg": {0: 0, 1: 0}}

        def preproc(w_src, dst, transp_src):
            for c in range(HC):
                stg = 1024 + (c % 2) * 1024
                for k in range(KC):
                    buf = k % 2
                    if transp_src:
                        src = w_src[128 * c:128 * (c + 1), 128 * k:128 * (k + 1)]
                    else:
                        src = w_src[128 * k:128 * (k + 1), 128 * c:128 * (c + 1)]
                    tps = (TP, TQ)[buf][0]
                    tpq = ("sync", "scalar")[buf]
                    p.wait("gpsimd", tps, pp["tp"][buf])
                    p.emit("gpsimd", lambda e, s2=src, b=buf: e.dma_start(
                        out=sb(b * 128, 128), in_=s2), GA[buf], 16)
                    p.wait(tpq, GA[buf])
                    p.wait(tpq, GS[c % 2], pp["stg"][c % 2])
                    p.emit(tpq, lambda e, b=buf, stg=stg, k=k: e.dma_start_transpose(
                        sb(stg + k * 128, 128), sb(b * 128, 128)), tps, 16)
                    pp["tp"][buf] = tps.n
                    pp["cast"][buf] = GA[buf].n
                p.wait("gpsimd", TP[0])
                p.wait("gpsimd", TQ[0])
                p.emit("gpsimd", lambda e, c=c, stg=stg, d=dst: e.dma_start(
                    out=d[c], in_=sb(stg, 1024)), GS[c % 2], 16)
                pp["stg"][c % 2] = GS[c % 2].n

        p.wait("vector", SC)
        preproc(wg_d, wgT_s, True)
        preproc(wu_d, wuT_s, True)
        preproc(wd_d, wdT_s, False)
        pre_ve = VE.n

        for s in range(nseq):
            if s > 0:
                for e in ENGS:
                    for sem in ALL_DMA + [MM, VE, SC]:
                        p.wait(e, sem)

            if s == 0:
                p.wait("sync", GS[0])
                p.wait("sync", GS[1])
                p.wait("tensor", VE)
            xg_done = [0] * NG
            for g in range(NG):
                if g > 0:
                    p.wait("sync", MM, xg_done[g - 1])
                for j in range(4):
                    i = g * 4 + j
                    p.emit("sync", lambda e, s=s, i=i, j=j: e.dma_start(
                        out=sl(B0 + j * 1024, 1024),
                        in_=xin_d[s, 128 * i:128 * (i + 1), :]), DX[g % 2], 16)
                p.wait("tensor", DX[g % 2])
                for k in range(KC):
                    bank = (pC, pD)[k % 2]
                    p.wait("tensor", VE, st["xev"][k % 2])
                    for j in range(4):
                        p.emit("tensor", lambda e, bk=bank, j=j, k=k: e.transpose(
                            bk[:, j * 128:(j + 1) * 128],
                            sl(B0 + j * 1024 + k * 128, 128), IDENT),
                            MM, 1 if j == 3 else None, kind="tp")
                    p.wait("vector", MM)
                    p.emit("vector", lambda e, k=k, g=g, bk=bank: e.tensor_copy(
                        sl(A0 + k * 2048 + g * 512, 512), bk[:]), VE, 1, kind="pr")
                    st["xev"][k % 2] = VE.n
                xg_done[g] = MM.n
            xphase_ve = VE.n
            if s == 0:
                snap("s1")

            p.wait("tensor", VE, xphase_ve)
            if s == 0:
                p.wait("sync", VE, pre_ve)
            rot = [0]
            vrot = [0]

            def qkv_slab(c, col_base, sbase):
                wbuf = sbase + (c % 2) * 1024
                p.wait("sync", MM, st["slab"][c % 2])
                for k in range(KC):
                    p.emit("sync", lambda e, k=k, c=c, wb=wbuf, cb=col_base: e.dma_start(
                        out=sl(wb + k * 128, 128),
                        in_=wqkv[128 * k:128 * (k + 1),
                                 cb + 128 * c:cb + 128 * (c + 1)]), DQ[c % 2], 16)
                return wbuf

            def qkv_mms(wbuf, g, bank, bank_ap=None):
                for k in range(KC):
                    ap = bank_ap if bank_ap is not None else (lambda b: b[:])
                    p.emit("tensor", lambda e, bk=bank, wb=wbuf, k=k, g=g, ap=ap:
                           e.matmul(ap(bk), sl(wb + k * 128, 128),
                                    sl(A0 + k * 2048 + g * 512, 512),
                                    start=(k == 0), stop=(k == KC - 1)),
                           MM, 1 if k == KC - 1 else None)

            for part, cb, sbase in (("K", D, C0), ("V", 2 * D, P0)):
                for c in range(KC):
                    wbuf = qkv_slab(c, cb, sbase)
                    p.wait("tensor", DQ[c % 2])
                    for g in range(NG):
                        r = rot[0] % 2
                        bank = (pA, pB)[r]
                        rot[0] += 1
                        p.wait("tensor", VE, st["mmfree"][r])
                        qkv_mms(wbuf, g, bank)
                        p.wait("vector", MM)
                        if part == "K":
                            p.emit("vector", lambda e, bk=bank, c=c, g=g: e.tensor_scalar(
                                sl(B0 + c * 2048 + g * 512, 512), bk[:],
                                sl(BQ0 + KC + c, 1), None, ALU.add), VE, 1, kind="pr")
                            st["mmfree"][r] = VE.n
                        else:
                            p.emit("vector", lambda e, bk=bank, c=c: e.tensor_scalar(
                                sl(TB, 512), bk[:], sl(BQ0 + 2 * KC + c, 1), None,
                                ALU.add), VE, 1, kind="pr")
                            st["mmfree"][r] = VE.n
                            p.wait("tensor", VE)
                            for j in range(4):
                                vb = vrot[0] % 2
                                bank2 = (pC, pD)[vb]
                                p.wait("tensor", SC, st["vev"][vb])
                                p.emit("tensor", lambda e, bk2=bank2, j=j: e.transpose(
                                    bk2[:, 0:128], sl(TB + j * 128, 128), IDENT), MM, 1,
                                    kind="tp")
                                p.wait("scalar", MM)
                                p.emit("scalar", lambda e, bk2=bank2, g=g, j=j, c=c:
                                       e.copy(sl(C0 + (g * 4 + j) * 1024 + c * 128, 128),
                                              bk2[:, 0:128]), SC, 1, kind="pr")
                                st["vev"][vb] = SC.n
                                vrot[0] += 1
                    st["slab"][c % 2] = MM.n
            p.wait("tensor", SC)
            for g in range(NG):
                for c in range(KC):
                    wbuf = qkv_slab(c, 0, P0)
                    p.wait("tensor", DQ[c % 2])
                    if c < 4:
                        bank, bap = pS, (lambda b, c=c: b[:, c * 512:(c + 1) * 512])
                    else:
                        bank, bap = (pA, pB, pC, pD)[c - 4], (lambda b: b[:])
                    p.wait("tensor", VE, st["qev"].get(("v", c), 0))
                    p.wait("tensor", SC, st["qev"].get(("s", c), 0))
                    qkv_mms(wbuf, g, bank, bap)
                    st["slab"][c % 2] = MM.n
                qmm = MM.n
                p.wait("vector", MM, qmm)
                p.wait("scalar", MM, qmm)
                for c in range(KC):
                    if c < 4:
                        src = pS[:, c * 512:(c + 1) * 512]
                    else:
                        src = (pA, pB, pC, pD)[c - 4][:]
                    if c % 2 == 0:
                        p.emit("vector", lambda e, src=src, c=c, g=g: e.tensor_scalar(
                            sl(A0 + c * 2048 + g * 512, 512), src,
                            sl(BQ0 + c, 1), None, ALU.add), VE, 1, kind="pr")
                        st["qev"][("v", c)] = VE.n
                    else:
                        p.emit("scalar", lambda e, src=src, c=c, g=g: e.activation(
                            sl(A0 + c * 2048 + g * 512, 512), src, AF.Identity,
                            bias=sl(BQ0 + c, 1), scale=1.0), SC, 1, kind="pr")
                        st["qev"][("s", c)] = SC.n
            qkv_ve = VE.n
            qkv_sc = SC.n
            if s == 0:
                snap("s2")

            p.wait("tensor", VE, qkv_ve)
            p.wait("tensor", SC, qkv_sc)
            for i in range(NT):
                kb = i // 4
                lo = (i % 4) * 128
                p.wait("tensor", SC, st["exp"])
                p.wait("tensor", VE, st["t1r"])
                for kg in range(4):
                    for k in range(KC):
                        p.emit("tensor", lambda e, kg=kg, k=k, i=i: e.matmul(
                            pS[:, kg * 512:(kg + 1) * 512],
                            sl(A0 + k * 2048 + i * 128, 128),
                            sl(B0 + k * 2048 + kg * 512, 512),
                            start=(k == 0), stop=(k == KC - 1)),
                            MM, 1 if (k == KC - 1 and kg == 3) else None)
                s_done = MM.n
                p.wait("vector", MM, s_done)
                p.wait("vector", SC, st["exp"])
                p.emit("vector", lambda e: e.reduce_max(sl(S_MNEG, 1), pS[:], AX,
                                                        negate=True), VE, 1, kind="pr")
                p.wait("vector", VE)
                p.emit("vector", lambda e: e.tensor_scalar(
                    sl(S_MNEG, 1), sl(S_MNEG, 1), INV_SQRT_D, 0.0, ALU.mult, ALU.min),
                    VE, 1)
                p.emit("vector", lambda e, i=i: e.scalar_tensor_tensor(
                    DIAG, pS[:, i * 128:(i + 1) * 128], INV_SQRT_D, MASK,
                    ALU.mult, ALU.mult), VE, 1, kind="pr")
                p.emit("vector", lambda e: e.memset(sl(S_ACC, 8), 0.0), VE, 1)
                stats_ve = VE.n
                st["t1r"] = stats_ve
                p.wait("scalar", VE, stats_ve)
                p.wait("scalar", VE, st["ptr_ve"])
                p.wait("scalar", MM, st["pv_mm"])
                na = 0
                for b in range(4):
                    if b != kb:
                        p.emit("scalar", lambda e, b=b, na=na: e.activation(
                            sl(P0 + b * 512, 512), pS[:, b * 512:(b + 1) * 512], AF.Exp,
                            bias=sl(S_MNEG, 1), scale=INV_SQRT_D,
                            accum_out=sl(S_ACC + na, 1)), SC, 1, kind="pr")
                        na += 1
                if lo > 0:
                    p.emit("scalar", lambda e, kb=kb, lo=lo, na=na: e.activation(
                        sl(P0 + kb * 512, lo), pS[:, kb * 512:kb * 512 + lo], AF.Exp,
                        bias=sl(S_MNEG, 1), scale=INV_SQRT_D,
                        accum_out=sl(S_ACC + na, 1)), SC, 1, kind="pr")
                    na += 1
                if lo < 384:
                    p.emit("scalar", lambda e, kb=kb, lo=lo, na=na: e.activation(
                        sl(P0 + kb * 512 + lo + 128, 384 - lo),
                        pS[:, kb * 512 + lo + 128:(kb + 1) * 512], AF.Exp,
                        bias=sl(S_MNEG, 1), scale=INV_SQRT_D,
                        accum_out=sl(S_ACC + na, 1)), SC, 1, kind="pr")
                    na += 1
                p.emit("scalar", lambda e, i=i, na=na: e.activation(
                    sl(P0 + i * 128, 128), DIAG, AF.Exp, bias=sl(S_MNEG, 1), scale=1.0,
                    accum_out=sl(S_ACC + na, 1)), SC, 1)
                st["exp"] = SC.n
                p.wait("vector", SC, st["exp"])
                p.emit("vector", lambda e: e.reduce_sum(sl(S_Z, 1), sl(S_ACC, 8), AX),
                       VE, 1)
                p.wait("vector", VE)
                p.emit("vector", lambda e: e.reciprocal(sl(S_RZ, 1), sl(S_Z, 1)), VE, 1)
                rz_ve = VE.n
                if s == 0 and i == 0:
                    snap("s3")
                p.wait("tensor", SC, st["exp"])
                for k in range(NT):
                    half = (k // 4) % 2
                    bank = (pC, pD)[half]
                    if k % 4 == 0:
                        p.wait("tensor", VE, st["ptev"][half])
                    p.emit("tensor", lambda e, bk=bank, k=k: e.transpose(
                        bk[:, (k % 4) * 128:(k % 4 + 1) * 128], sl(P0 + k * 128, 128),
                        IDENT), MM, 1 if k % 4 == 3 else None, kind="tp")
                    if k % 4 == 3:
                        p.wait("vector", MM)
                        p.emit("vector", lambda e, bk=bank, k=k: e.tensor_copy(
                            sl(P0 + (k - 3) * 128, 512), bk[:]), VE, 1, kind="pr")
                        st["ptev"][half] = VE.n
                st["ptr_ve"] = VE.n
                ptev_ve = VE.n
                p.wait("tensor", VE, ptev_ve)
                for dg in range(2):
                    bank = (pA, pB)[dg]
                    p.wait("tensor", VE, st["opv"][dg])
                    for k in range(NT):
                        p.emit("tensor", lambda e, bk=bank, k=k, dg=dg: e.matmul(
                            bk[:], sl(P0 + k * 128, 128),
                            sl(C0 + k * 1024 + dg * 512, 512),
                            start=(k == 0), stop=(k == NT - 1)),
                            MM, 1 if k == NT - 1 else None)
                st["pv_mm"] = MM.n
                p.wait("vector", MM, st["pv_mm"])
                p.wait("vector", VE, rz_ve)
                p.wait("vector", MM, st["ot_mm"])
                for dg in range(2):
                    bank = (pA, pB)[dg]
                    p.emit("vector", lambda e, bk=bank, dg=dg: e.tensor_scalar(
                        sl(TB + dg * 512, 512), bk[:], sl(S_RZ, 1), None, ALU.mult),
                        VE, 1, kind="pr")
                    st["opv"][dg] = VE.n
                oev_ve = VE.n
                if s == 0 and i == 0:
                    snap("s4")
                p.wait("tensor", VE, oev_ve)
                for k in range(KC):
                    bank = (pC, pD)[k % 2]
                    p.wait("tensor", VE, st["otv_v"][k % 2])
                    p.wait("tensor", SC, st["otv_s"][k % 2])
                    p.wait("tensor", VE, st["ptev"][k % 2])
                    p.emit("tensor", lambda e, bk=bank, k=k: e.transpose(
                        bk[:, 0:128], sl(TB + k * 128, 128), IDENT), MM, 1, kind="tp")
                    if k % 2 == 0:
                        p.wait("vector", MM)
                        p.emit("vector", lambda e, bk=bank, k=k, i=i: e.tensor_copy(
                            sl(A0 + k * 2048 + i * 128, 128), bk[:, 0:128]), VE, 1,
                            kind="pr")
                        st["otv_v"][k % 2] = VE.n
                    else:
                        p.wait("scalar", MM)
                        p.emit("scalar", lambda e, bk=bank, k=k, i=i: e.copy(
                            sl(A0 + k * 2048 + i * 128, 128), bk[:, 0:128]), SC, 1,
                            kind="pr")
                        st["otv_s"][k % 2] = SC.n
                st["ot_mm"] = MM.n
            att_mm = MM.n
            att_ve = VE.n
            att_sc = SC.n

            p.wait("sync", MM, att_mm)
            for k in range(KC):
                p.emit("sync", lambda e, k=k: e.dma_start(
                    out=sl(PB_WP + k * 1024, 1024),
                    in_=wproj[128 * k:128 * (k + 1), :]), L0, 16)
            p.emit("sync", lambda e: e.dma_start(out=BPROW, in_=bpr_d), L0, 16)
            wproj_ld = L0.n
            p.wait("tensor", SC, att_sc)
            p.wait("tensor", VE, att_ve)
            xin_free = {0: 0, 1: 0}
            for i in range(NT):
                xb = i % 2
                p.wait("sync", VE, xin_free[xb])
                p.emit("sync", lambda e, s=s, i=i, xb=xb: e.dma_start(
                    out=sl(PB_XIN + xb * 1024, 1024),
                    in_=xin_d[s, 128 * i:128 * (i + 1), :]), DPX[xb], 16)
                xld = DPX[xb].n
                p.wait("tensor", L0, wproj_ld)
                for og in range(2):
                    bank = (pA, pB)[og]
                    p.wait("tensor", VE, st["prj"][og])
                    for k in range(KC):
                        p.emit("tensor", lambda e, bk=bank, k=k, i=i, og=og: e.matmul(
                            bk[:], sl(A0 + k * 2048 + i * 128, 128),
                            sl(PB_WP + k * 1024 + og * 512, 512),
                            start=(k == 0), stop=False))
                    p.emit("tensor", lambda e, bk=bank, og=og: e.matmul(
                        bk[:], ONES, BPROW[0:1, og * 512:(og + 1) * 512],
                        start=False, stop=True), MM, 1)
                r_mm = MM.n
                p.wait("vector", MM, r_mm)
                p.wait("vector", DPX[xb], xld)
                p.wait("vector", VE, st.get("lnap", 0))
                for og in range(2):
                    bank = (pA, pB)[og]
                    p.emit("vector", lambda e, bk=bank, og=og, xb=xb:
                           e.scalar_tensor_tensor(
                               sl(PB_U + og * 512, 512), bk[:], 1.0,
                               sl(PB_XIN + xb * 1024 + og * 512, 512),
                               ALU.mult, ALU.add, accum_out=sl(S_AU + og, 1)), VE, 1,
                           kind="pr")
                    st["prj"][og] = VE.n
                xin_free[xb] = VE.n
                u_ve = VE.n
                _ln_stats_apply(p, sl, PB_U, PB_USQ, u_ve, VE, SC,
                                lambda e, i=i: e.tensor_scalar(
                                    sl(C0 + i * 1024, 1024), sl(PB_U, 1024),
                                    sl(S_MEAN, 1), sl(S_RSTD, 1),
                                    ALU.subtract, ALU.mult))
                st["lnap"] = VE.n
            proj_ve = VE.n
            proj_mm = MM.n
            proj_sc = SC.n
            if s == 0:
                snap("s5")

            p.wait("sync", MM, proj_mm)
            for tg in range(NG):
                for e2 in ENGS:
                    for sem2 in ALL_DMA + [MM, VE, SC]:
                        p.wait(e2, sem2)
                p.wait("tensor", VE, proj_ve if tg == 0 else st["ffn_ve"])
                p.wait("tensor", SC, proj_sc if tg == 0 else st["ffn_sc"])
                gu_prev = st["gu_mm"]
                if TPX_EXCL:
                    for _sem in (GY[0], GY[1], DG[0], DG[1], DU[0], DU[1],
                                 DD[0], DD[1], DPX[0], DPX[1]):
                        p.wait("sync", _sem)
                        p.wait("scalar", _sem)
                for qt in range(4):
                    ti = tg * 4 + qt
                    p.wait("vector", TP[1], st["x2s"][0])
                    p.wait("vector", TQ[1], st["x2s"][1])
                    p.wait("vector", VE, st.get("lnap", 0))
                    p.emit("vector", lambda e, ti=ti: e.tensor_copy(
                        sb(FH_SH, 1024), sl(C0 + ti * 1024, 1024)), VE, 1)
                    p.emit("vector", lambda e, ti=ti: e.scalar_tensor_tensor(
                        sb(FH_SL, 1024), sl(C0 + ti * 1024, 1024), 1.0,
                        sb(FH_SH, 1024), ALU.mult, ALU.subtract), VE, 1)
                    spl = VE.n
                    p.wait("sync", VE, spl)
                    p.wait("scalar", VE, spl)
                    p.wait("sync", MM, gu_prev)
                    p.wait("scalar", MM, gu_prev)
                    for k in range(KC):
                        p.emit("sync", lambda e, k=k, qt=qt: e.dma_start_transpose(
                            sa(AH_XTH + k * 512 + qt * 128, 128),
                            sb(FH_SH + k * 128, 128)), TP[1], 16)
                        p.emit("scalar", lambda e, k=k, qt=qt: e.dma_start_transpose(
                            sa(AH_XTL + k * 512 + qt * 128, 128),
                            sb(FH_SL + k * 128, 128)), TQ[1], 16)
                    st["x2s"] = (TP[1].n, TQ[1].n)
                p.wait("tensor", TP[1])
                p.wait("tensor", TQ[1])
                if TPX_EXCL:
                    p.wait("sync", TP[1])
                    p.wait("sync", TQ[1])
                    p.wait("gpsimd", TP[1])
                    p.wait("gpsimd", TQ[1])
                for c in range(HC):
                    wb = c % 2
                    p.wait("sync", MM, st["fslab"][wb])
                    p.emit("sync", lambda e, c=c, wb=wb: e.dma_start(
                        out=sb(FH_WG + wb * 1024, 1024), in_=wgT_s[c]), DG[wb], 16)
                    p.emit("sync", lambda e, c=c, wb=wb: e.dma_start(
                        out=sb(FH_WU + wb * 1024, 1024), in_=wuT_s[c]), DU[wb], 16)
                    p.wait("tensor", DG[wb])
                    p.wait("tensor", DU[wb])
                    p.wait("tensor", SC, st["gate"])
                    for pc in range(2):
                        xoff = (AH_XTH, AH_XTL)[pc]
                        for k in range(KC):
                            p.emit("tensor", lambda e, wb=wb, k=k, xo=xoff, pc=pc:
                                   e.matmul(
                                pA[:], sb(FH_WG + wb * 1024 + k * 128, 128),
                                sa(xo + k * 512, 512),
                                start=(pc == 0 and k == 0),
                                stop=(pc == 1 and k == KC - 1)),
                                MM, 1 if (pc == 1 and k == KC - 1) else None)
                    gmm = MM.n
                    p.wait("tensor", VE, st["up"])
                    for pc in range(2):
                        xoff = (AH_XTH, AH_XTL)[pc]
                        for k in range(KC):
                            p.emit("tensor", lambda e, wb=wb, k=k, xo=xoff, pc=pc:
                                   e.matmul(
                                pB[:], sb(FH_WU + wb * 1024 + k * 128, 128),
                                sa(xo + k * 512, 512),
                                start=(pc == 0 and k == 0),
                                stop=(pc == 1 and k == KC - 1)),
                                MM, 1 if (pc == 1 and k == KC - 1) else None)
                    st["fslab"][wb] = MM.n
                    st["gu_mm"] = MM.n
                    umm = MM.n
                    gsb = c % 2
                    p.wait("scalar", MM, gmm)
                    p.wait("scalar", VE, st["hgs"][gsb])
                    p.emit("scalar", lambda e, gsb=gsb: e.activation(
                        sl(FB_GS + gsb * 512, 512), pA[:], AF.Silu,
                        scale=1.0 / 127.0), SC, 1, kind="pr")
                    st["gate"] = SC.n
                    p.wait("vector", MM, umm)
                    p.wait("vector", SC, st["gate"])
                    p.wait("vector", MM, st["h32r"][gsb])
                    p.emit("vector", lambda e, gsb=gsb: e.scalar_tensor_tensor(
                        sl(FB_H32 + gsb * 512, 512), pB[:], 1.0 / 127.0,
                        sl(FB_GS + gsb * 512, 512), ALU.mult, ALU.mult), VE, 1,
                        kind="pr")
                    st["up"] = VE.n
                    p.wait("vector", VE)
                    p.emit("vector", lambda e, c=c, gsb=gsb: e.tensor_copy(
                        sa(AH_HHI + c * 512, 512), sl(FB_H32 + gsb * 512, 512)), VE, 1)
                    p.wait("vector", VE)
                    p.emit("vector", lambda e, c=c, gsb=gsb: e.scalar_tensor_tensor(
                        sa(AH_HLO + c * 512, 512), sl(FB_H32 + gsb * 512, 512), 1.0,
                        sa(AH_HHI + c * 512, 512), ALU.mult, ALU.subtract), VE, 1)
                    st["hgs"][gsb] = VE.n
                h_ve = VE.n
                if s == 0 and tg == 0:
                    snap("s6")
                p.wait("tensor", VE, h_ve)
                for c in range(HC):
                    wb = c % 2
                    p.wait("sync", MM, st["dslab"][wb])
                    p.emit("sync", lambda e, c=c, wb=wb: e.dma_start(
                        out=sb(FH_WD + wb * 1024, 1024), in_=wdT_s[c]), DD[wb], 16)
                    p.wait("tensor", DD[wb])
                    for qt in range(4):
                        for dg in range(2):
                            for pc in range(2):
                                hoff = (AH_HHI, AH_HLO)[pc]
                                p.emit("tensor",
                                       lambda e, c=c, wb=wb, qt=qt, dg=dg, ho=hoff,
                                       pc=pc:
                                       e.matmul(
                                           (pS[:, qt * 512:(qt + 1) * 512] if dg == 0
                                            else (pA, pB, pC, pD)[qt][:]),
                                           sa(ho + c * 512 + qt * 128, 128),
                                           sb(FH_WD + wb * 1024 + dg * 512, 512),
                                           start=(c == 0 and pc == 0),
                                           stop=(c == HC - 1 and pc == 1),
                                           skip_group_check=True),
                                       MM, 1 if (qt == 3 and dg == 1 and pc == 1)
                                       else None)
                    st["dslab"][wb] = MM.n
                down_mm = MM.n
                p.wait("vector", MM, down_mm)
                for qt in range(4):
                    ti = tg * 4 + qt
                    p.wait("vector", VE, st.get("lnap2", 0))
                    for dg in range(2):
                        src = (pS[:, qt * 512:(qt + 1) * 512] if dg == 0 else
                               (pA, pB, pC, pD)[qt][:])
                        p.emit("vector", lambda e, src=src, qt=qt, dg=dg, ti=ti:
                               e.scalar_tensor_tensor(
                                   sl(FB_U2 + qt * 1024 + dg * 512, 512), src,
                                   1.0 / 127.0,
                                   sl(C0 + ti * 1024 + dg * 512, 512),
                                   ALU.mult, ALU.add, accum_out=sl(S_AU + dg, 1)),
                               VE, 1, kind="pr")
                    u2_ve = VE.n
                    yb = qt % 2
                    p.wait("vector", GY[yb], st["yst"][yb])

                    def apply_y(e, qt=qt, yb=yb):
                        return e.tensor_scalar(
                            sl(FB_YS + yb * 1024, 1024), sl(FB_U2 + qt * 1024, 1024),
                            sl(S_MEAN, 1), sl(S_RSTD, 1), ALU.subtract, ALU.mult)

                    _ln_stats_apply(p, sl, FB_U2 + qt * 1024, FB_USQ, u2_ve, VE, SC,
                                    apply_y)
                    st["lnap2"] = VE.n
                    p.wait("gpsimd", VE)
                    p.emit("gpsimd", lambda e, s=s, ti=ti, yb=yb: e.dma_start(
                        out=y_d[s, 128 * ti:128 * (ti + 1), :],
                        in_=sl(FB_YS + yb * 1024, 1024)), GY[yb], 16)
                    st["yst"][yb] = GY[yb].n
                st["ffn_ve"] = VE.n
                st["ffn_sc"] = SC.n

        for sem in (GY[0], GY[1], GS[0], GS[1], SDBG):
            p.wait("gpsimd", sem)

        @blk.sync
        def _(eng):
            for fn in p.q["sync"]:
                fn(eng)

        @blk.gpsimd
        def _(eng):
            for fn in p.q["gpsimd"]:
                fn(eng)

        @blk.scalar
        def _(eng):
            for fn in p.q["scalar"]:
                fn(eng)

        @blk.vector
        def _(eng):
            for fn in p.q["vector"]:
                fn(eng)

        @blk.tensor
        def _(eng):
            for fn in p.q["tensor"]:
                fn(eng)

    return nc


def kernel(L0_attn_x, L0_attn_Wqkv, L0_attn_bqkv, L0_attn_Wproj, L0_attn_bproj,
           L0_attn_gamma, L0_ffn_w_gate, L0_ffn_w_up, L0_ffn_w_down):
    x = np.ascontiguousarray(np.asarray(L0_attn_x, dtype=np.float32))
    wqkv = np.ascontiguousarray(np.asarray(L0_attn_Wqkv, dtype=np.float32))
    bq = np.asarray(L0_attn_bqkv, dtype=np.float32)
    wp = np.ascontiguousarray(np.asarray(L0_attn_Wproj, dtype=np.float32))
    bp = np.asarray(L0_attn_bproj, dtype=np.float32)
    gam = np.asarray(L0_attn_gamma, dtype=np.float32).reshape(1, 1)
    wg = np.ascontiguousarray(np.asarray(L0_ffn_w_gate, dtype=np.int32))
    wu = np.ascontiguousarray(np.asarray(L0_ffn_w_up, dtype=np.int32))
    wd = np.ascontiguousarray(np.asarray(L0_ffn_w_down, dtype=np.int32))

    bq_pp = np.ascontiguousarray(bq.reshape(HC, 128).T)
    bpr = np.ascontiguousarray(bp.reshape(1, D))
    ident = np.eye(128, dtype=np.float32)
    ii = np.arange(128, dtype=np.float32)
    diff = (ii[:, None] - ii[None, :]).astype(np.float32)
    tri = (diff >= 0).astype(np.float32)
    ones_r = np.ones((1, 128), dtype=np.float32)

    nc = build(NSEQ)
    per = x.shape[0] // NCORE
    in_maps = []
    for c in range(NCORE):
        in_maps.append({
            "x_in": x[c * per:(c + 1) * per],
            "wqkv": wqkv, "bq_pp": bq_pp, "wproj": wp, "bproj_row": bpr,
            "gamma2": gam, "wg_i": wg, "wu_i": wu, "wd_i": wd,
            "ident_c": ident, "diff_c": diff, "tri_c": tri, "ones_c": ones_r,
        })
    res = run_bass_kernel_spmd(nc, in_maps, core_ids=list(range(NCORE)))
    out = np.concatenate([r["y"] for r in res.results], axis=0)
    return np.ascontiguousarray(out.astype(np.float32))
